# version 04 (final)
# baseline (speedup 1.0000x reference)
import numpy as np
from contextlib import ExitStack

import concourse.bass as bass
import concourse.mybir as mybir
from concourse.bass_utils import run_bass_kernel_spmd

F32 = mybir.dt.float32
BF16 = mybir.dt.bfloat16
AF = mybir.ActivationFunctionType
ALU = mybir.AluOpType

P = 128
DM = 2048
INNER = 4096
CH = 2048
NCG = CH // P
NST = 16
L = 4096
LH = 2048
WIN = 1024
NWIN = L // WIN
DTR = 128
NEXP = 8
FF = 1408
NKD = DM // P
NKF = FF // P

ENGINES = ("sync", "tensor", "vector", "scalar", "gpsimd")


class Prog:

    def __init__(self, nc):
        self.nc = nc
        self.recs = {e: [] for e in ENGINES}
        self.cnt = {}
        self.sems = {}
        self._es = ExitStack()

    def sem(self, name):
        h = self._es.enter_context(self.nc.semaphore(name))
        self.sems[name] = h
        self.cnt[name] = 0
        return h

    def op(self, eng, fn, waits=(), inc=None, inc_by=1):
        after = None
        if inc is not None:
            self.cnt[inc] += (inc_by if inc_by is not None else 1)
            after = self.cnt[inc]
        waits = tuple((self.sems[s], v) for (s, v) in waits)
        incs = (self.sems[inc], inc_by) if inc is not None else None
        self.recs[eng].append((waits, fn, incs))
        return after

    def barrier(self, dummy=None):
        self.cnt["bar"] += len(ENGINES)
        tgt = self.cnt["bar"]
        barsem = self.sems["bar"]
        dswait = ((self.sems["ds"], self.cnt["ds"]),)
        dgwait = ((self.sems["dg"], self.cnt["dg"]),)
        for e in ENGINES:
            if e in ("vector", "scalar", "gpsimd") and dummy is not None:
                sl = {"vector": 0, "scalar": 1, "gpsimd": 2}[e]
                if e == "scalar":
                    def _f(eng, sl_=sl, d_=dummy):
                        return eng.memzero(d_[0:1, sl_ * 4:(sl_ + 1) * 4])
                else:
                    def _f(eng, sl_=sl, d_=dummy):
                        return eng.memset(d_[0:1, sl_ * 4:(sl_ + 1) * 4], 0.0)
            else:
                def _f(eng):
                    return eng.nop()
            w = dswait if e == "sync" else (dgwait if e == "gpsimd" else ())
            self.recs[e].append((w, _f, (barsem, 1)))
            self.recs[e].append((((barsem, tgt),), None, None))

    def run_blocks(self, block):
        def mk(eng_name):
            def body(e):
                for waits, fn, incs in self.recs[eng_name]:
                    for (h, v) in waits:
                        e.wait_ge(h, v)
                    inst = fn(e) if fn is not None else None
                    if incs is not None:
                        h, by = incs
                        if inst is None:
                            inst = e.nop()
                        if by is None:
                            inst.then_inc(h)
                        else:
                            inst.then_inc(h, by)
            return body
        block.sync(mk("sync"))
        block.tensor(mk("tensor"))
        block.vector(mk("vector"))
        block.scalar(mk("scalar"))
        block.gpsimd(mk("gpsimd"))


def build_program(dev_dumps=(), phases=("A", "A2", "B", "C"), a_level=99):
    nc = bass.Bass("TRN2", target_bir_lowering=False, debug=False,
                   detect_race_conditions=False)
    es = ExitStack()

    def dram_in(name, shape, dtype=F32):
        return nc.dram_tensor(name, shape, dtype, kind="ExternalInput").ap()

    def dram_out(name, shape, dtype=F32):
        return nc.dram_tensor(name, shape, dtype, kind="ExternalOutput").ap()

    def dram_scratch(name, shape, dtype):
        kind = "ExternalOutput" if name in dev_dumps else "Internal"
        return nc.dram_tensor(name, shape, dtype, kind=kind).ap()

    xT = dram_in("xT", [DM, L])
    xT_res = dram_in("xT_res", [DM, LH])
    w_in_T = dram_in("w_in_T", [DM, 2 * CH])
    w_xp_T = dram_in("w_xp_T", [CH, 160])
    w_dt_T = dram_in("w_dt_T", [DTR, CH])
    w_out_T = dram_in("w_out_T", [CH, DM])
    gate_T = dram_in("gate_T", [DM, NEXP])
    w1 = dram_in("w1", [NEXP, DM, 2 * FF])
    w2 = dram_in("w2", [NEXP, FF, DM])
    sw1_T = dram_in("sw1_T", [DM, FF])
    sw2_T = dram_in("sw2_T", [DM, FF])
    sw3_T = dram_in("sw3_T", [FF, DM])
    v_dtb = dram_in("v_dtb", [P, NCG])
    v_convb = dram_in("v_convb", [P, NCG])
    v_convw = dram_in("v_convw", [P, NCG * 4])
    v_alog = dram_in("v_alog", [P, NCG * NST])
    v_D = dram_in("v_D", [P, NCG])

    out_fm = dram_out("out_fm", [DM, LH], F32)

    w1s = dram_scratch("w1s", [NEXP, DM, 2 * FF], BF16)
    w2s = dram_scratch("w2s", [NEXP, FF, DM], BF16)
    sw1s = dram_scratch("sw1s", [DM, FF], BF16)
    sw2s = dram_scratch("sw2s", [DM, FF], BF16)
    sw3s = dram_scratch("sw3s", [FF, DM], BF16)
    u_scr = dram_scratch("u_scr", [CH, L], BF16)
    sz_scr = dram_scratch("sz_scr", [CH, L], BF16)
    ccA_in = dram_scratch("ccA_in", [160, L], F32)
    ccA_out = dram_scratch("ccA_out", [160, L], F32)
    ccB_in = [dram_scratch(f"ccB_in{j}", [2 * DM, WIN], BF16) for j in range(2)]
    ccB_out = [dram_scratch(f"ccB_out{j}", [DM, WIN], BF16) for j in range(2)]
    dbg_xdbl = dram_scratch("dbg_xdbl", [160, L], F32) if "dbg_xdbl" in dev_dumps else None
    dbg_delta = dram_scratch("dbg_delta", [CH, L], BF16) if "dbg_delta" in dev_dumps else None
    dbg_y = dram_scratch("dbg_y", [CH, L], BF16) if "dbg_y" in dev_dumps else None
    dbg_h1 = dram_scratch("dbg_h1", [DM, LH], BF16) if "dbg_h1" in dev_dumps else None
    dbg_mam = dram_scratch("dbg_mam", [DM, LH], BF16) if "dbg_mam" in dev_dumps else None
    dbg_ffn = dram_scratch("dbg_ffn", [DM, LH], BF16) if "dbg_ffn" in dev_dumps else None
    dbg_cw = dram_scratch("dbg_cw", [P, 512], F32) if "dbg_cw" in dev_dumps else None
    dbg_moe = dram_scratch("dbg_moe", [DM, LH], BF16) if "dbg_moe" in dev_dumps else None

    def sb(name, shape, dtype):
        return es.enter_context(nc.sbuf_tensor(name, shape, dtype))

    def ps(name, shape, dtype=F32):
        return es.enter_context(nc.psum_tensor(name, shape, dtype))

    dummy = sb("bar_dummy", [1, 12], F32)
    ones_col = sb("ones_col", [P, 1], BF16)
    ones_row = sb("ones_row", [1, P], BF16)
    ones_row32 = sb("ones_row32", [1, P], F32)
    eps_col = sb("eps_col", [P, 1], F32)
    ident32 = sb("ident32", [P, P], F32)
    selcw = sb("selcw", [NEXP, NEXP * P], BF16)
    sel32 = sb("sel32", [32, 32 * P], BF16)
    hcarry = sb("hcarry", [P, NCG * NST], F32)
    negA = sb("negA", [P, NCG * NST], F32)
    t_dtb = sb("t_dtb", [P, NCG], F32)
    t_convb = sb("t_convb", [P, NCG], F32)
    t_convw = sb("t_convw", [P, NCG * 4], F32)
    t_D = sb("t_D", [P, NCG], F32)
    flush_row = sb("flush_row", [1, 512], BF16)
    esAB = ExitStack()
    w_dt_bf = esAB.enter_context(nc.sbuf_tensor("w_dt_bf", [DTR, CH], BF16))
    dt_bf = esAB.enter_context(nc.sbuf_tensor("dt_bf", [DTR, L], BF16))
    BC_bf = esAB.enter_context(nc.sbuf_tensor("BC_bf", [32, L], BF16))

    pr = Prog(nc)
    for s in ("dg", "ds", "t", "v", "s", "g", "bar", "cc0", "cc1", "cc2"):
        pr.sem(s)

    W = pr.cnt

    pp_flush = es.enter_context(nc.psum_tensor("pp_flush", [1, 512], F32))

    def pe_flush():
        return pr.op("tensor", lambda e: e.matmul(
            pp_flush[0:1, 0:512], ones_row[0:1, 0:1], flush_row[0:1, 0:512],
            start=True, stop=True), inc="t")

    def em_prologue():
        for src, dst in ((v_dtb, t_dtb), (v_convb, t_convb), (v_convw, t_convw),
                         (v_D, t_D)):
            pr.op("gpsimd", lambda e, s_=src, d_=dst: e.dma_start(out=d_[:], in_=s_),
                  inc="dg", inc_by=16)
        pr.op("gpsimd", lambda e: e.dma_start(out=negA[:], in_=v_alog), inc="dg", inc_by=16)
        pr.op("gpsimd", lambda e: e.dma_start(out=w_dt_bf[:], in_=w_dt_T), inc="dg", inc_by=16)
        dg_after = W["dg"]
        pr.op("vector", lambda e: e.memset(ones_col[:], 1.0))
        pr.op("vector", lambda e: e.memset(ones_row[:], 1.0))
        pr.op("vector", lambda e: e.memset(ones_row32[:], 1.0))
        pr.op("vector", lambda e: e.memset(eps_col[:], 1e-5))
        pr.op("vector", lambda e: e.memset(hcarry[:], 0.0))
        pr.op("vector", lambda e: e.memset(flush_row[:], 0.0))
        pr.op("gpsimd", lambda e: e.memset(ident32[:], 0.0))
        pr.op("gpsimd", lambda e: e.affine_select(
            out=ident32[:], in_=ident32[:], compare_op=ALU.not_equal, fill=1.0,
            base=0, pattern=[[-1, P]], channel_multiplier=1))
        pr.op("gpsimd", lambda e: e.memset(selcw[:], 0.0))
        for m in range(NEXP):
            pr.op("gpsimd", lambda e, m_=m: e.affine_select(
                out=selcw[:, m_ * P:(m_ + 1) * P], in_=selcw[:, m_ * P:(m_ + 1) * P],
                compare_op=ALU.not_equal, fill=1.0,
                base=-m_, pattern=[[0, P]], channel_multiplier=1))
        pr.op("gpsimd", lambda e: e.memset(sel32[:], 0.0))
        for m in range(32):
            pr.op("gpsimd", lambda e, m_=m: e.affine_select(
                out=sel32[:, m_ * P:(m_ + 1) * P], in_=sel32[:, m_ * P:(m_ + 1) * P],
                compare_op=ALU.not_equal, fill=1.0,
                base=-m_, pattern=[[0, P]], channel_multiplier=1))
        pr.op("gpsimd", lambda e: e.memset(dummy[0:1, 8:12], 0.0), inc="g")
        pr.op("scalar", lambda e: e.activation(out=negA[:], in_=negA[:], func=AF.Exp),
              waits=[("dg", dg_after)], inc="s")
        pr.op("vector", lambda e: e.tensor_scalar_mul(negA[:], negA[:], -1.0),
              waits=[("s", W["s"])], inc="v")

    def em_phaseA():
        esA = ExitStack()
        def sbA(name, shape, dtype):
            return esA.enter_context(nc.sbuf_tensor(name, shape, dtype))
        TH = L // 2
        NTB = TH // 512
        xT_bf = sbA("xT_bf", [P, NKD * TH], BF16)
        wspan = [sbA(f"wspanA{i}", [P, 16 * 1024], BF16) for i in range(2)]
        u_pre = sbA("u_pre", [P, TH + 3], F32)
        acc_c = sbA("acc_c", [P, TH], F32)
        u_bfc = sbA("u_bfc", [P, TH], BF16)
        sz_bfc = sbA("sz_bfc", [P, TH], BF16)
        xT_edge = sbA("xT_edge", [P, NKD * 4], BF16)
        pp = [esA.enter_context(nc.psum_tensor(f"ppA{i}", [P, 512], F32)) for i in range(4)]

        NSPAN = 4
        wspan_ready = [0, 0]
        wspan_lastuse = [0, 0]
        pp_free = [0, 0]
        pp_evac = [0, 0]

        for th in range(2):
            t0 = th * TH
            t_now = W["t"]
            for k in range(NKD):
                ld_x = pr.op("gpsimd",
                             lambda e, k_=k, t0_=t0: e.dma_start(
                                 out=xT_bf[:, k_ * TH:(k_ + 1) * TH],
                                 in_=xT[k_ * P:(k_ + 1) * P, t0_:t0_ + TH]),
                             waits=[("t", t_now)], inc="dg", inc_by=16)
            if th == 1:
                for k in range(NKD):
                    ld_x = pr.op("gpsimd",
                                 lambda e, k_=k, t0_=t0: e.dma_start(
                                     out=xT_edge[:, k_ * 4:k_ * 4 + 3],
                                     in_=xT[k_ * P:(k_ + 1) * P, t0_ - 3:t0_]),
                                 inc="dg", inc_by=16)
            for sp in range(NSPAN):
                buf = sp % 2
                wlu = wspan_lastuse[buf]
                for k in range(NKD):
                    wspan_ready[buf] = pr.op(
                        "gpsimd",
                        lambda e, k_=k, sp_=sp, b_=buf: e.dma_start(
                            out=wspan[b_][:, k_ * 1024:(k_ + 1) * 1024],
                            in_=w_in_T[k_ * P:(k_ + 1) * P, sp_ * 1024:(sp_ + 1) * 1024]),
                        waits=[("t", wlu)], inc="dg", inc_by=16)
                for ec in range(8):
                    echunk = sp * 8 + ec
                    is_u = echunk < NCG
                    cg = echunk if is_u else echunk - NCG
                    if a_level < 2:
                        continue
                    for tb in range(NTB):
                        b = tb % 2
                        def mm_group(e, sp_=sp, ec_=ec, tb_=tb, b_=b, buf_=buf):
                            last = None
                            for k in range(NKD):
                                last = e.matmul(pp[b_][:, 0:512],
                                         wspan[buf_][:, k * 1024 + ec_ * P:k * 1024 + ec_ * P + P],
                                         xT_bf[:, k * TH + tb_ * 512:k * TH + (tb_ + 1) * 512],
                                         start=(k == 0), stop=(k == NKD - 1))
                            return last
                        pr.op("tensor", mm_group,
                              waits=[("dg", max(wspan_ready[buf], ld_x)),
                                     ("s", pp_evac[b])],
                              inc="t")
                        pp_free[b] = W["t"]
                        if a_level == 2.5:
                            pr.op("scalar", lambda e: e.nop(),
                                  waits=[("t", pp_free[b])], inc="s")
                        elif is_u or a_level == 2.7:
                            pr.op("scalar",
                                  lambda e, tb_=tb, b_=b: e.copy(
                                      out=u_pre[:, 3 + tb_ * 512: 3 + (tb_ + 1) * 512],
                                      in_=pp[b_][:, 0:512]),
                                  waits=[("t", pp_free[b])], inc="s")
                        else:
                            pr.op("scalar",
                                  lambda e, tb_=tb, b_=b: e.activation(
                                      out=sz_bfc[:, tb_ * 512:(tb_ + 1) * 512],
                                      in_=pp[b_][:, 0:512], func=AF.Silu),
                                  waits=[("t", pp_free[b])], inc="s")
                        pp_evac[b] = W["s"]
                    if a_level < 3:
                        pr.op("scalar", lambda e: e.nop(),
                              waits=[("ds", W["ds"]), ("v", W["v"])], inc="s")
                        pp_evac[0] = W["s"]
                        pp_evac[1] = W["s"]
                        wspan_lastuse[buf] = W["t"]
                        continue
                    if is_u:
                        if th == 0:
                            pr.op("vector", lambda e: e.memset(u_pre[:, 0:3], 0.0),
                                  waits=[("s", pp_evac[0]), ("s", pp_evac[1])], inc="v")
                        else:
                            def mm_edge(e, sp_=sp, ec_=ec, buf_=buf):
                                last = None
                                for k in range(NKD):
                                    last = e.matmul(pp[0][:, 0:3],
                                             wspan[buf_][:, k * 1024 + ec_ * P:k * 1024 + ec_ * P + P],
                                             xT_edge[:, k * 4:k * 4 + 3],
                                             start=(k == 0), stop=(k == NKD - 1))
                                return last
                            pr.op("tensor", mm_edge,
                                  waits=[("s", pp_evac[0])], inc="t")
                            pe_flush()
                            pr.op("scalar",
                                  lambda e: e.copy(out=u_pre[:, 0:3], in_=pp[0][:, 0:3]),
                                  waits=[("t", W["t"]), ("s", pp_evac[1])], inc="s")
                            pp_evac[0] = W["s"]
                            pr.op("vector", lambda e: e.nop(),
                                  waits=[("s", W["s"])], inc="v")
                        pr.op("vector",
                              lambda e, cg_=cg: e.tensor_scalar(
                                  out=acc_c[:], in0=u_pre[:, 0:TH],
                                  scalar1=t_convw[:, cg_ * 4:cg_ * 4 + 1], scalar2=None,
                                  op0=ALU.mult),
                              waits=[("s", pp_evac[0]), ("s", pp_evac[1])], inc="v")
                        for k in range(1, 4):
                            pr.op("vector",
                                  lambda e, k_=k, cg_=cg: e.scalar_tensor_tensor(
                                      out=acc_c[:], in0=u_pre[:, k_:TH + k_],
                                      scalar=t_convw[:, cg_ * 4 + k_:cg_ * 4 + k_ + 1],
                                      in1=acc_c[:], op0=ALU.mult, op1=ALU.add),
                                  inc="v")
                        v_conv = W["v"]
                        pr.op("scalar",
                              lambda e, cg_=cg: e.activation(
                                  out=u_bfc[:], in_=acc_c[:], func=AF.Silu,
                                  bias=t_convb[:, cg_:cg_ + 1]),
                              waits=[("v", v_conv)], inc="s")
                        s_silu = W["s"]
                        pr.op("sync",
                              lambda e, cg_=cg, t0_=t0: e.dma_start(
                                  out=u_scr[cg_ * P:(cg_ + 1) * P, t0_:t0_ + TH],
                                  in_=u_bfc[:]),
                              waits=[("s", s_silu)], inc="ds", inc_by=16)
                    else:
                        pr.op("sync",
                              lambda e, cg_=cg, t0_=t0: e.dma_start(
                                  out=sz_scr[cg_ * P:(cg_ + 1) * P, t0_:t0_ + TH],
                                  in_=sz_bfc[:]),
                              waits=[("s", pp_evac[0]), ("s", pp_evac[1])],
                              inc="ds", inc_by=16)
                    pr.op("scalar", lambda e: e.nop(),
                          waits=[("ds", W["ds"]), ("v", W["v"])], inc="s")
                    pp_evac[0] = W["s"]
                    pp_evac[1] = W["s"]
                    wspan_lastuse[buf] = W["t"]
        esA.close()
        pr.barrier(dummy)

    def em_phaseA2():
        esA = ExitStack()
        def sbA(name, shape, dtype):
            return esA.enter_context(nc.sbuf_tensor(name, shape, dtype))
        xp_lhsT = sbA("xp_lhsT", [P, NCG * 160], BF16)
        u_mv = [sbA(f"u_mvA2{i}", [P, 512], BF16) for i in range(2)]
        xd_dt = sbA("xd_dt", [P, L], F32)
        xd_bc = sbA("xd_bc", [32, L], F32)
        p_dt = esA.enter_context(nc.psum_tensor("pA2dt", [P, 512], F32))
        p_bc = esA.enter_context(nc.psum_tensor("pA2bc", [32, 512], F32))

        for k in range(NCG):
            ldw = pr.op("gpsimd", lambda e, k_=k: e.dma_start(
                out=xp_lhsT[:, k_ * 160:(k_ + 1) * 160],
                in_=w_xp_T[k_ * P:(k_ + 1) * P, :]),
                inc="dg", inc_by=16)
        u_ready = [0, 0]
        u_freed = [0, 0]
        for tb in range(L // 512):
            for k in range(NCG):
                b = k % 2
                u_ready[b] = pr.op("gpsimd",
                      lambda e, k_=k, tb_=tb, b_=b: e.dma_start(
                          out=u_mv[b_][:], in_=u_scr[k_ * P:(k_ + 1) * P,
                                                     tb_ * 512:(tb_ + 1) * 512]),
                      waits=[("t", u_freed[b])], inc="dg", inc_by=16)
                def mmx(e, k_=k, b_=b):
                    e.matmul(p_dt[:, 0:512], xp_lhsT[:, k_ * 160:k_ * 160 + 128],
                             u_mv[b_][:], start=(k_ == 0), stop=(k_ == NCG - 1))
                    return e.matmul(p_bc[:, 0:512], xp_lhsT[:, k_ * 160 + 128:(k_ + 1) * 160],
                             u_mv[b_][:], start=(k_ == 0), stop=(k_ == NCG - 1))
                pr.op("tensor", mmx,
                      waits=[("dg", max(u_ready[b], ldw)), ("v", W["v"])], inc="t")
                u_freed[b] = W["t"]
            pr.op("vector", lambda e, tb_=tb: e.tensor_copy(
                out=xd_dt[:, tb_ * 512:(tb_ + 1) * 512], in_=p_dt[:, 0:512]),
                waits=[("t", W["t"] + 1)], inc="v")
            pr.op("vector", lambda e, tb_=tb: e.tensor_copy(
                out=xd_bc[:, tb_ * 512:(tb_ + 1) * 512], in_=p_bc[:, 0:512]),
                inc="v")
            pe_flush()
        v_done = W["v"]
        pr.op("sync", lambda e: e.dma_start(out=ccA_in[0:128, :], in_=xd_dt[:]),
              waits=[("v", v_done)], inc="ds", inc_by=16)
        pr.op("sync", lambda e: e.dma_start(out=ccA_in[128:160, :], in_=xd_bc[:]),
              inc="ds", inc_by=16)
        ds_done = W["ds"]
        pr.op("gpsimd", lambda e: e.collective_compute(
            "AllReduce", ALU.add, [[0, 1], [2, 3], [4, 5], [6, 7]],
            [ccA_in.opt()], [ccA_out.opt()]),
            waits=[("ds", ds_done)], inc="cc0", inc_by=None)
        pr.op("gpsimd", lambda e: e.dma_start(out=dt_bf[:], in_=ccA_out[0:128, :]),
              waits=[("cc0", 1)], inc="dg", inc_by=16)
        pr.op("gpsimd", lambda e: e.dma_start(out=BC_bf[:], in_=ccA_out[128:160, :]),
              inc="dg", inc_by=16)
        if dbg_xdbl is not None:
            pr.op("sync", lambda e: e.dma_start(out=dbg_xdbl[0:128, :], in_=xd_dt[:]),
                  waits=[("cc0", 1)], inc="ds", inc_by=16)
            pr.op("sync", lambda e: e.dma_start(out=dbg_xdbl[128:160, :], in_=xd_bc[:]),
                  inc="ds", inc_by=16)
        esA.close()
        pr.barrier(dummy)

    def em_phaseB():
        esB = ExitStack()
        def sbB(name, shape, dtype):
            return esB.enter_context(nc.sbuf_tensor(name, shape, dtype))
        Dw = sbB("Dw", [P, NCG * WIN], BF16)
        Du = sbB("Du", [P, NCG * WIN], BF16)
        y_w = sbB("y_w", [P, NCG * WIN], BF16)
        w_all = sbB("w_all", [P, NCG * WIN], BF16)
        dAb = [sbB(f"dAb{i}", [P, 2 * WIN], F32) for i in range(2)]
        bcB = [sbB(f"bcB{i}", [P, WIN], BF16) for i in range(2)]
        bcC = [sbB(f"bcC{i}", [P, WIN], BF16) for i in range(2)]
        t1_t = sbB("t1_t", [P, WIN], F32)
        t2_t = sbB("t2_t", [P, WIN], F32)
        u_t = [sbB(f"u_tB{i}", [P, WIN], BF16) for i in range(2)]
        sz_t = [sbB(f"sz_tB{i}", [P, WIN], BF16) for i in range(2)]
        yd_t = sbB("yd_t", [P, WIN], F32)
        p8 = [esB.enter_context(nc.psum_tensor(f"p8B{i}", [P, 512], F32)) for i in range(7)]

        dg_base = W["dg"]
        cast_jobs = []
        for ex in range(NEXP):
            cast_jobs.append((w1s[ex], w1[ex]))
            cast_jobs.append((w2s[ex], w2[ex]))
        cast_jobs += [(sw1s, sw1_T), (sw2s, sw2_T), (sw3s, sw3_T)]
        for w in range(NWIN):
            t0 = w * WIN
            u_ready = [0, 0]
            u_freed = [0, 0]
            for cg in range(NCG):
                bb = cg % 2
                u_ready[bb] = pr.op("sync",
                      lambda e, cg_=cg, t0_=t0, b_=bb: e.dma_start(
                          out=u_t[b_][:], in_=u_scr[cg_ * P:(cg_ + 1) * P, t0_:t0_ + WIN]),
                      waits=[("v", u_freed[bb])], inc="ds", inc_by=16)
                def mm_dt(e, cg_=cg, t0_=t0):
                    e.matmul(p8[0][:, 0:512], w_dt_bf[:, cg_ * P:(cg_ + 1) * P],
                             dt_bf[:, t0_:t0_ + 512], start=True, stop=True)
                    return e.matmul(p8[1][:, 0:512], w_dt_bf[:, cg_ * P:(cg_ + 1) * P],
                             dt_bf[:, t0_ + 512:t0_ + 1024], start=True, stop=True)
                pr.op("tensor", mm_dt, waits=[("s", W["s"]), ("dg", dg_base)], inc="t")
                pe_flush()
                tflush = W["t"]
                DWS = Dw[:, cg * WIN:(cg + 1) * WIN]
                pr.op("scalar", lambda e, cg_=cg, d_=DWS: e.activation(
                    out=d_[:, 0:512], in_=p8[0][:, 0:512], func=AF.Identity,
                    bias=t_dtb[:, cg_:cg_ + 1]), waits=[("t", tflush), ("v", W["v"])],
                    inc="s")
                pr.op("scalar", lambda e, cg_=cg, d_=DWS: e.activation(
                    out=d_[:, 512:1024], in_=p8[1][:, 0:512], func=AF.Identity,
                    bias=t_dtb[:, cg_:cg_ + 1]), inc="s")
                pr.op("scalar", lambda e, d_=DWS: e.activation(
                    out=t1_t[:], in_=d_, func=AF.Abs), inc="s")
                pr.op("scalar", lambda e: e.activation(
                    out=t2_t[:], in_=t1_t[:], func=AF.Exp, scale=-1.0), inc="s")
                pr.op("scalar", lambda e: e.activation(
                    out=t2_t[:], in_=t2_t[:], func=AF.Ln, bias=1.0), inc="s")
                s_ln = W["s"]
                pr.op("vector", lambda e, d_=DWS: e.tensor_scalar_max(t1_t[:], d_, 0.0),
                      waits=[("s", s_ln)], inc="v")
                pr.op("vector", lambda e, d_=DWS: e.tensor_tensor(
                    out=d_, in0=t1_t[:], in1=t2_t[:],
                    op=ALU.add), inc="v")
                pr.op("vector", lambda e, cg_=cg, b_=bb: e.tensor_tensor(
                    out=Du[:, cg_ * WIN:(cg_ + 1) * WIN],
                    in0=Dw[:, cg_ * WIN:(cg_ + 1) * WIN], in1=u_t[b_][:],
                    op=ALU.mult), waits=[("ds", u_ready[bb])], inc="v")
                u_freed[bb] = W["v"]
            pre_v = W["v"]
            bc_ready = [0, 0]
            bc_freed = [0, 0]
            def emit_bc(n):
                nb = n % 2
                def mm_bc(e, n_=n, t0_=t0, nb_=nb):
                    for half in range(2):
                        e.matmul(p8[2 + half][:, 0:512],
                                 sel32[:, n_ * P:(n_ + 1) * P],
                                 BC_bf[:, t0_ + half * 512:t0_ + (half + 1) * 512],
                                 start=True, stop=True)
                    for half in range(2):
                        last = e.matmul(p8[4 + half][:, 0:512],
                                 sel32[:, (16 + n_) * P:(17 + n_) * P],
                                 BC_bf[:, t0_ + half * 512:t0_ + (half + 1) * 512],
                                 start=True, stop=True)
                    return last
                pr.op("tensor", mm_bc, waits=[("v", bc_freed[nb]), ("s", W["s"]),
                                              ("dg", dg_base)], inc="t")
                pe_flush()
                tf = W["t"]
                pr.op("scalar", lambda e, nb_=nb: e.copy(out=bcB[nb_][:, 0:512], in_=p8[2][:, 0:512]),
                      waits=[("t", tf), ("v", bc_freed[nb])], inc="s")
                pr.op("scalar", lambda e, nb_=nb: e.copy(out=bcB[nb_][:, 512:1024], in_=p8[3][:, 0:512]), inc="s")
                pr.op("scalar", lambda e, nb_=nb: e.copy(out=bcC[nb_][:, 0:512], in_=p8[4][:, 0:512]), inc="s")
                pr.op("scalar", lambda e, nb_=nb: e.copy(out=bcC[nb_][:, 512:1024], in_=p8[5][:, 0:512]), inc="s")
                bc_ready[nb] = W["s"]
            emit_bc(0)
            dA_freed = [0, 0]
            for n in range(NST):
                nb = n % 2
                pr.op("vector", lambda e, nb_=nb: e.tensor_tensor(
                    out=w_all[:].rearrange("p (c t) -> p c t", c=NCG),
                    in0=Du[:].rearrange("p (c t) -> p c t", c=NCG),
                    in1=bcB[nb_][:].unsqueeze(1).broadcast_to([P, NCG, WIN]),
                    op=ALU.mult), waits=[("s", bc_ready[nb])], inc="v")
                if n + 1 < NST:
                    emit_bc(n + 1)
                for g in range(NCG // 2):
                    pb = g % 2
                    def dA_pair(e, g_=g, n_=n, pb_=pb):
                        for j in range(2):
                            cg = 2 * g_ + j
                            last = e.activation(
                                out=dAb[pb_][:, j * WIN:(j + 1) * WIN],
                                in_=Dw[:, cg * WIN:(cg + 1) * WIN], func=AF.Exp,
                                scale=negA[:, cg * NST + n_:cg * NST + n_ + 1])
                        return last
                    dA_ready = pr.op("scalar", dA_pair,
                          waits=[("v", max(dA_freed[pb], pre_v))], inc="s")
                    def scan_pair(e, g_=g, n_=n, pb_=pb):
                        for j in range(2):
                            cg = 2 * g_ + j
                            last = e.tensor_tensor_scan(
                                out=w_all[:, cg * WIN:(cg + 1) * WIN],
                                data0=dAb[pb_][:, j * WIN:(j + 1) * WIN],
                                data1=w_all[:, cg * WIN:(cg + 1) * WIN],
                                initial=hcarry[:, cg * NST + n_:cg * NST + n_ + 1],
                                op0=ALU.mult, op1=ALU.add)
                        return last
                    pr.op("vector", scan_pair, waits=[("s", dA_ready)], inc="v")
                    dA_freed[pb] = W["v"]
                pr.op("vector", lambda e, n_=n: e.tensor_copy(
                    out=hcarry[:, n_::NST],
                    in_=w_all[:].rearrange("p (c t) -> p c t", c=NCG)[:, :, WIN - 1]),
                    inc="v")
                pr.op("vector", lambda e, nb_=nb: e.tensor_tensor(
                    out=w_all[:].rearrange("p (c t) -> p c t", c=NCG),
                    in0=w_all[:].rearrange("p (c t) -> p c t", c=NCG),
                    in1=bcC[nb_][:].unsqueeze(1).broadcast_to([P, NCG, WIN]),
                    op=ALU.mult), inc="v")
                bc_freed[nb] = W["v"]
                if n == 0:
                    pr.op("vector", lambda e: e.tensor_copy(out=y_w[:], in_=w_all[:]),
                          inc="v")
                else:
                    pr.op("vector", lambda e: e.tensor_tensor(
                        out=y_w[:], in0=y_w[:], in1=w_all[:], op=ALU.add), inc="v")
            vscan_done = W["v"]
            if dbg_y is not None:
                for cg in range(NCG):
                    pr.op("sync", lambda e, cg_=cg, t0_=t0: e.dma_start(
                        out=dbg_y[cg_ * P:(cg_ + 1) * P, t0_:t0_ + WIN],
                        in_=y_w[:, cg_ * WIN:(cg_ + 1) * WIN]),
                        waits=[("v", vscan_done)], inc="ds", inc_by=16)
            ds_dump = W["ds"]
            u_ready = [0, 0]; sz_ready = [0, 0]; pair_freed = [0, 0]
            for cg in range(NCG):
                bb = cg % 2
                u_ready[bb] = pr.op("sync",
                      lambda e, cg_=cg, t0_=t0, b_=bb: e.dma_start(
                          out=u_t[b_][:], in_=u_scr[cg_ * P:(cg_ + 1) * P, t0_:t0_ + WIN]),
                      waits=[("v", max(pair_freed[bb], vscan_done)), ("ds", ds_dump)],
                      inc="ds", inc_by=16)
                sz_ready[bb] = pr.op("sync",
                      lambda e, cg_=cg, t0_=t0, b_=bb: e.dma_start(
                          out=sz_t[b_][:], in_=sz_scr[cg_ * P:(cg_ + 1) * P, t0_:t0_ + WIN]),
                      inc="ds", inc_by=16)
                pr.op("vector", lambda e, cg_=cg, b_=bb: e.scalar_tensor_tensor(
                    out=yd_t[:], in0=u_t[b_][:], scalar=t_D[:, cg_:cg_ + 1],
                    in1=y_w[:, cg_ * WIN:(cg_ + 1) * WIN], op0=ALU.mult, op1=ALU.add),
                    waits=[("ds", sz_ready[bb])], inc="v")
                pr.op("vector", lambda e, cg_=cg, b_=bb: e.tensor_tensor(
                    out=y_w[:, cg_ * WIN:(cg_ + 1) * WIN], in0=yd_t[:], in1=sz_t[b_][:],
                    op=ALU.mult), inc="v")
                pair_freed[bb] = W["v"]
            y2_v = W["v"]
            hw, jj = w // 2, w % 2
            for dmh in range(2):
                wready = 0
                for k in range(NCG):
                    wready = pr.op("gpsimd", lambda e, k_=k, d_=dmh: e.dma_start(
                        out=w_all[:, k_ * WIN:(k_ + 1) * WIN],
                        in_=w_out_T[k_ * P:(k_ + 1) * P, d_ * 1024:(d_ + 1) * 1024]),
                        waits=[("v", y2_v), ("s", W["s"])],
                        inc="dg", inc_by=16)
                for grp in ((0, 1, 2), (3, 4, 5), (6, 7)):
                    def mm_out(e, grp_=grp):
                        for gi, dm in enumerate(grp_):
                            for k in range(NCG):
                                for Nb in range(2):
                                    last = e.matmul(
                                        p8[2 * gi + Nb][:, 0:512],
                                        w_all[:, k * WIN + dm * P:k * WIN + (dm + 1) * P],
                                        y_w[:, k * WIN + Nb * 512:k * WIN + (Nb + 1) * 512],
                                        start=(k == 0), stop=(k == NCG - 1))
                        return last
                    pr.op("tensor", mm_out,
                          waits=[("dg", wready), ("v", y2_v), ("s", W["s"])], inc="t")
                    pe_flush()
                    tf2 = W["t"]
                    for gi, dm in enumerate(grp):
                        for Nb in range(2):
                            pr.op("scalar", lambda e, d_=dmh, dm_=dm, gi_=gi, Nb_=Nb: e.copy(
                                out=Du[:, (d_ * 8 + dm_) * WIN + Nb_ * 512:
                                          (d_ * 8 + dm_) * WIN + Nb_ * 512 + 512],
                                in_=p8[2 * gi_ + Nb_][:, 0:512]),
                                waits=([("t", tf2), ("v", y2_v)] if gi == 0 and Nb == 0
                                       else [("t", tf2)]),
                                inc="s")
            s_outp = W["s"]
            for dm in range(NCG):
                pr.op("sync", lambda e, dm_=dm, hw_=hw, jj_=jj: e.dma_start(
                    out=ccB_in[jj_][hw_ * DM + dm_ * P: hw_ * DM + (dm_ + 1) * P, :],
                    in_=Du[:, dm_ * WIN:(dm_ + 1) * WIN]),
                    waits=[("s", s_outp)], inc="ds", inc_by=16)
            ncast = 7 if w < 2 else len(cast_jobs) if w == 3 else 5
            for _ in range(min(ncast, len(cast_jobs))):
                dst, srcap = cast_jobs.pop(0)
                pr.op("gpsimd", lambda e, d_=dst, s_=srcap: e.dma_start(out=d_, in_=s_),
                      inc="dg", inc_by=16)
            if w >= 2:
                ds_now = W["ds"]
                ccs = "cc1" if jj == 0 else "cc2"
                pr.op("gpsimd", lambda e, jj_=jj: e.collective_compute(
                    "ReduceScatter", ALU.add, [[0, 1], [2, 3], [4, 5], [6, 7]],
                    [ccB_in[jj_].opt()], [ccB_out[jj_].opt()]),
                    waits=[("ds", ds_now)], inc=ccs, inc_by=None)
                if dbg_mam is not None:
                    pr.op("sync", lambda e, jj_=jj: e.dma_start(
                        out=dbg_mam[:, jj_ * WIN:(jj_ + 1) * WIN], in_=ccB_out[jj_][:, :]),
                        waits=[(ccs, 1)], inc="ds", inc_by=16)
        nonlocal_cast = W["dg"]
        em_phaseB.cast_done = nonlocal_cast
        esB.close()
        pr.barrier(dummy)

    def em_phaseC():
        cast_done = getattr(em_phaseB, "cast_done", 0)
        esC = ExitStack()
        def sbC(name, shape, dtype):
            return esC.enter_context(nc.sbuf_tensor(name, shape, dtype))
        h1b = sbC("h1b", [P, NKD * 1024], BF16)
        accb = sbC("accb", [P, NKD * 1024], BF16)
        wbuf = sbC("wbuf", [P, NCG * FF], BF16)
        sga = sbC("sga", [P, NKF * 1024], BF16)
        cwbc = sbC("cwbc", [P, 1024], BF16)
        mam_k = [sbC(f"mamC{i}", [P, 1024], BF16) for i in range(2)]
        xres_k = [sbC(f"xresC{i}", [P, 1024], F32) for i in range(2)]
        sq_k = sbC("sqC", [P, 1024], BF16)
        rows = sbC("rowsC", [1, 6 * 1024], F32)
        a_bc = sbC("a_bcC", [P, 1024], F32)
        b_bc = sbC("b_bcC", [P, 1024], F32)
        glog = sbC("glogC", [P, 64], F32)
        gsc = sbC("gscC", [P, 64], F32)
        gm8 = sbC("gm8C", [P, 64], F32)
        gr1 = sbC("gr1C", [P, 8], F32)
        gr2 = sbC("gr2C", [P, 8], F32)
        gr3 = sbC("gr3C", [P, 8], F32)
        gmask = sbC("gmaskC", [P, 64], F32)
        cw_t = sbC("cw_tC", [P, 64], F32)
        cwT = sbC("cwTC", [8, 1024], BF16)
        gate_bf = sbC("gate_bfC", [P, NKD * 8], BF16)
        of32 = sbC("of32C", [P, 1024], F32)
        p8 = [esC.enter_context(nc.psum_tensor(f"p8C{i}", [P, 512], F32)) for i in range(7)]

        ROW = lambda r: rows[0:1, r * 1024:(r + 1) * 1024]

        for k in range(NKD):
            pr.op("gpsimd", lambda e, k_=k: e.dma_start(
                out=gate_bf[:, k_ * 8:(k_ + 1) * 8],
                in_=gate_T[k_ * P:(k_ + 1) * P, :]), inc="dg", inc_by=16)
        gate_ld = W["dg"]

        def emit_ln(src_getk, dst_write, j):
            for k in range(NKD):
                sqv = W["s"]
                pr.op("scalar", lambda e, k_=k: e.activation(
                    out=sq_k[:], in_=src_getk(k_), func=AF.Square),
                    waits=[("v", W["v"]), ("t", W["t"])], inc="s")
                s_sq = W["s"]
                def mm_st(e, k_=k):
                    for Nb in range(2):
                        e.matmul(p8[Nb][0:1, 0:512], ones_col[:],
                                 src_getk(k_)[:, Nb * 512:(Nb + 1) * 512],
                                 start=(k_ == 0), stop=(k_ == NKD - 1))
                    for Nb in range(2):
                        last = e.matmul(p8[2 + Nb][0:1, 0:512], ones_col[:],
                                 sq_k[:, Nb * 512:(Nb + 1) * 512],
                                 start=(k_ == 0), stop=(k_ == NKD - 1))
                    return last
                pr.op("tensor", mm_st, waits=[("s", s_sq)], inc="t")
            pe_flush()
            tfl = W["t"]
            pr.op("scalar", lambda e: e.activation(
                out=ROW(0)[:, 0:512], in_=p8[0][0:1, 0:512], func=AF.Identity,
                scale=1.0 / DM), waits=[("t", tfl), ("v", W["v"])], inc="s")
            pr.op("scalar", lambda e: e.activation(
                out=ROW(0)[:, 512:1024], in_=p8[1][0:1, 0:512], func=AF.Identity,
                scale=1.0 / DM), inc="s")
            pr.op("scalar", lambda e: e.activation(
                out=ROW(1)[:, 0:512], in_=p8[2][0:1, 0:512], func=AF.Identity,
                scale=1.0 / DM), inc="s")
            pr.op("scalar", lambda e: e.activation(
                out=ROW(1)[:, 512:1024], in_=p8[3][0:1, 0:512], func=AF.Identity,
                scale=1.0 / DM), inc="s")
            s_rows = W["s"]
            pr.op("vector", lambda e: e.tensor_tensor(
                out=ROW(2), in0=ROW(0), in1=ROW(0), op=ALU.mult),
                waits=[("s", s_rows)], inc="v")
            pr.op("vector", lambda e: e.tensor_tensor(
                out=ROW(2), in0=ROW(1), in1=ROW(2), op=ALU.subtract), inc="v")
            v_var = W["v"]
            pr.op("scalar", lambda e: e.activation(
                out=ROW(3), in_=ROW(2), func=AF.Ln, bias=eps_col[0:1, 0:1]),
                waits=[("v", v_var)], inc="s")
            pr.op("scalar", lambda e: e.activation(
                out=ROW(3), in_=ROW(3), func=AF.Exp, scale=-0.5), inc="s")
            s_rstd = W["s"]
            pr.op("vector", lambda e: e.tensor_tensor(
                out=ROW(4), in0=ROW(0), in1=ROW(3), op=ALU.mult),
                waits=[("s", s_rstd)], inc="v")
            pr.op("vector", lambda e: e.tensor_scalar_mul(ROW(4), ROW(4), -1.0), inc="v")
            v_rows = W["v"]
            def mm_brc(e):
                for Nb in range(2):
                    e.matmul(p8[4][:, 0:512], ones_row[:],
                             ROW(3)[:, Nb * 512:(Nb + 1) * 512],
                             start=(Nb == 0), stop=False) if False else None
                e.matmul(p8[4][:, 0:512], ones_row[:], ROW(3)[:, 0:512],
                         start=True, stop=True)
                e.matmul(p8[5][:, 0:512], ones_row[:], ROW(3)[:, 512:1024],
                         start=True, stop=True)
                e.matmul(p8[6][:, 0:512], ones_row[:], ROW(4)[:, 0:512],
                         start=True, stop=True)
                return e.matmul(p8[6][:, 0:512], ones_row[:], ROW(4)[:, 512:1024],
                         start=True, stop=True) if False else e.matmul(
                         pp_flush[0:1, 0:512], ones_row[0:1, 0:1], flush_row[0:1, 0:512],
                         start=True, stop=True)
            def mm_brc1(e):
                e.matmul(p8[4][:, 0:512], ones_row32[:], ROW(3)[:, 0:512],
                         start=True, stop=True)
                e.matmul(p8[5][:, 0:512], ones_row32[:], ROW(3)[:, 512:1024],
                         start=True, stop=True)
                e.matmul(p8[6][:, 0:512], ones_row32[:], ROW(4)[:, 0:512],
                         start=True, stop=True)
                return e.matmul(p8[0][:, 0:512], ones_row32[:], ROW(4)[:, 512:1024],
                         start=True, stop=True)
            pr.op("tensor", mm_brc1, waits=[("v", v_rows), ("s", W["s"])], inc="t")
            pe_flush()
            tbr = W["t"]
            pr.op("scalar", lambda e: e.copy(out=a_bc[:, 0:512], in_=p8[4][:, 0:512]),
                  waits=[("t", tbr)], inc="s")
            pr.op("scalar", lambda e: e.copy(out=a_bc[:, 512:1024], in_=p8[5][:, 0:512]), inc="s")
            pr.op("scalar", lambda e: e.copy(out=b_bc[:, 0:512], in_=p8[6][:, 0:512]), inc="s")
            pr.op("scalar", lambda e: e.copy(out=b_bc[:, 512:1024], in_=p8[0][:, 0:512]), inc="s")
            s_bc2 = W["s"]
            for k in range(NKD):
                dst_write(k, s_bc2)

        for j in range(2):
            mam_rdy = [0, 0]
            for k in range(NKD):
                bb = k % 2
                ccs = "cc1" if j == 0 else "cc2"
                mam_rdy[bb] = pr.op("sync", lambda e, k_=k, j_=j, b_=bb: e.dma_start(
                    out=mam_k[b_][:], in_=ccB_out[j_][k_ * P:(k_ + 1) * P, :]),
                    waits=[(ccs, 1), ("v", W["v"])], inc="ds", inc_by=16)
                pr.op("sync", lambda e, k_=k, j_=j, b_=bb: e.dma_start(
                    out=xres_k[b_][:], in_=xT_res[k_ * P:(k_ + 1) * P,
                                                  j_ * 1024:(j_ + 1) * 1024]),
                    inc="ds", inc_by=16)
                ds_k = W["ds"]
                pr.op("vector", lambda e, k_=k, b_=bb: e.tensor_tensor(
                    out=h1b[:, k_ * 1024:(k_ + 1) * 1024], in0=xres_k[b_][:],
                    in1=mam_k[b_][:], op=ALU.add),
                    waits=[("ds", ds_k)], inc="v")
            def ln1_write(k, s_bc2):
                pr.op("vector", lambda e, k_=k: e.tensor_tensor(
                    out=h1b[:, k_ * 1024:(k_ + 1) * 1024],
                    in0=h1b[:, k_ * 1024:(k_ + 1) * 1024], in1=a_bc[:], op=ALU.mult),
                    waits=[("s", s_bc2)] if k == 0 else [], inc="v")
                pr.op("vector", lambda e, k_=k: e.tensor_tensor(
                    out=h1b[:, k_ * 1024:(k_ + 1) * 1024],
                    in0=h1b[:, k_ * 1024:(k_ + 1) * 1024], in1=b_bc[:], op=ALU.add),
                    inc="v")
            emit_ln(lambda k: h1b[:, k * 1024:(k + 1) * 1024], ln1_write, j)
            h1_v = W["v"]
            if dbg_h1 is not None:
                for k in range(NKD):
                    pr.op("sync", lambda e, k_=k, j_=j: e.dma_start(
                        out=dbg_h1[k_ * P:(k_ + 1) * P, j_ * 1024:(j_ + 1) * 1024],
                        in_=h1b[:, k_ * 1024:(k_ + 1) * 1024]),
                        waits=[("v", h1_v)], inc="ds", inc_by=16)
            for i in range(8):
                def mm_gate(e, i_=i):
                    for k in range(NKD):
                        last = e.matmul(p8[1][:, 0:8],
                                 h1b[:, k * 1024 + i_ * P:k * 1024 + (i_ + 1) * P],
                                 gate_bf[:, k * 8:(k + 1) * 8],
                                 start=(k == 0), stop=(k == NKD - 1))
                    return last
                pr.op("tensor", mm_gate,
                      waits=[("v", h1_v), ("s", W["s"]), ("dg", gate_ld)], inc="t")
                pe_flush()
                tg = W["t"]
                pr.op("scalar", lambda e, i_=i: e.copy(
                    out=glog[:, i_ * 8:(i_ + 1) * 8], in_=p8[1][0:P, 0:8]),
                    waits=[("t", tg)], inc="s")
            s_gate = W["s"]
            G3 = lambda t: t[:].rearrange("p (i e) -> p i e", i=8)
            def vop(fn):
                pr.op("vector", fn, waits=[("v", W["v"]), ("s", W["s"])], inc="v")
            def sop(fn):
                pr.op("scalar", fn, waits=[("v", W["v"]), ("s", W["s"])], inc="s")
            vop(lambda e: e.reduce_max(
                out=gr1[:], in_=G3(glog), axis=mybir.AxisListType.X))
            for i in range(8):
                vop(lambda e, i_=i: e.tensor_single_scalar(
                    out=gsc[:, i_ * 8:(i_ + 1) * 8], in_=glog[:, i_ * 8:(i_ + 1) * 8],
                    scalar=gr1[:, i_:i_ + 1], op=ALU.subtract))
            sop(lambda e: e.activation(out=gsc[:], in_=gsc[:], func=AF.Exp))
            vop(lambda e: e.reduce_sum(
                out=gr1[:], in_=G3(gsc), axis=mybir.AxisListType.X))
            vop(lambda e: e.reciprocal(out=gr1[:], in_=gr1[:]))
            for i in range(8):
                vop(lambda e, i_=i: e.tensor_single_scalar(
                    out=gsc[:, i_ * 8:(i_ + 1) * 8], in_=gsc[:, i_ * 8:(i_ + 1) * 8],
                    scalar=gr1[:, i_:i_ + 1], op=ALU.mult))
            for i in range(8):
                vop(lambda e, i_=i: e.max(
                    out=gm8[:, i_ * 8:(i_ + 1) * 8], in_=gsc[:, i_ * 8:(i_ + 1) * 8]))
            vop(lambda e: e.tensor_tensor(
                out=gr1[:], in0=gm8[:, 0::8], in1=gm8[:, 1::8], op=ALU.add))
            vop(lambda e: e.tensor_scalar_add(gr1[:], gr1[:], 1e-20))
            vop(lambda e: e.reciprocal(out=gr1[:], in_=gr1[:]))
            vop(lambda e: e.tensor_tensor(
                out=gr2[:], in0=gm8[:, 0::8], in1=gr1[:], op=ALU.mult))
            vop(lambda e: e.tensor_tensor(
                out=gr3[:], in0=gm8[:, 1::8], in1=gr1[:], op=ALU.mult))
            for i in range(8):
                vop(lambda e, i_=i: e.tensor_scalar(
                    out=cw_t[:, i_ * 8:(i_ + 1) * 8], in0=gsc[:, i_ * 8:(i_ + 1) * 8],
                    scalar1=gm8[:, i_ * 8:i_ * 8 + 1], scalar2=gr2[:, i_:i_ + 1],
                    op0=ALU.is_equal, op1=ALU.mult))
                vop(lambda e, i_=i: e.tensor_scalar(
                    out=gmask[:, i_ * 8:(i_ + 1) * 8], in0=gsc[:, i_ * 8:(i_ + 1) * 8],
                    scalar1=gm8[:, i_ * 8 + 1:i_ * 8 + 2], scalar2=gr3[:, i_:i_ + 1],
                    op0=ALU.is_equal, op1=ALU.mult))
            vop(lambda e: e.tensor_tensor(
                out=cw_t[:], in0=cw_t[:], in1=gmask[:], op=ALU.add))
            v_cw = W["v"]
            for i in range(8):
                pr.op("tensor", lambda e, i_=i: e.transpose(
                    p8[1][0:8, 0:P], cw_t[:, i_ * 8:(i_ + 1) * 8], ident32[:]),
                    waits=[("v", v_cw), ("s", W["s"])], inc="t")
                pe_flush()
                tt_ = W["t"]
                pr.op("scalar", lambda e, i_=i: e.copy(
                    out=cwT[0:8, i_ * P:(i_ + 1) * P], in_=p8[1][0:8, 0:P]),
                    waits=[("t", tt_)], inc="s")
            if dbg_cw is not None:
                vcw0 = W["v"]
                pr.op("sync", lambda e, j_=j: e.dma_start(
                    out=dbg_cw[:, j_ * 256:j_ * 256 + 64], in_=glog[:]),
                    waits=[("v", vcw0)], inc="ds", inc_by=16)
                pr.op("sync", lambda e, j_=j: e.dma_start(
                    out=dbg_cw[:, j_ * 256 + 64:j_ * 256 + 128], in_=gsc[:]),
                    inc="ds", inc_by=16)
                pr.op("sync", lambda e, j_=j: e.dma_start(
                    out=dbg_cw[:, j_ * 256 + 128:j_ * 256 + 192], in_=gm8[:]),
                    inc="ds", inc_by=16)
                pr.op("sync", lambda e, j_=j: e.dma_start(
                    out=dbg_cw[:, j_ * 256 + 192:j_ * 256 + 256], in_=cw_t[:]),
                    inc="ds", inc_by=16)
            pr.op("vector", lambda e: e.memset(accb[:], 0.0), inc="v")
            for ex in range(NEXP + 1):
                shared = (ex == NEXP)
                if shared and dbg_moe is not None:
                    vmo = W["v"]
                    for k in range(NKD):
                        pr.op("sync", lambda e, k_=k, j_=j: e.dma_start(
                            out=dbg_moe[k_ * P:(k_ + 1) * P, j_ * 1024:(j_ + 1) * 1024],
                            in_=accb[:, k_ * 1024:(k_ + 1) * 1024]),
                            waits=[("v", vmo)], inc="ds", inc_by=16)
                if not shared:
                    def mm_cw(e, ex_=ex):
                        e.matmul(p8[6][:, 0:512], selcw[:, ex_ * P:(ex_ + 1) * P],
                                 cwT[0:8, 0:512], start=True, stop=True)
                        return e.matmul(p8[5][:, 0:512], selcw[:, ex_ * P:(ex_ + 1) * P],
                                 cwT[0:8, 512:1024], start=True, stop=True)
                    pr.op("tensor", mm_cw, waits=[("s", W["s"]), ("v", W["v"])], inc="t")
                    pe_flush()
                    tcw = W["t"]
                    pr.op("scalar", lambda e: e.copy(out=cwbc[:, 0:512], in_=p8[6][:, 0:512]),
                          waits=[("t", tcw), ("v", W["v"])], inc="s")
                    pr.op("scalar", lambda e: e.copy(out=cwbc[:, 512:1024], in_=p8[5][:, 0:512]),
                          inc="s")
                wrdy = 0
                for k in range(NKD):
                    if shared:
                        srcap = sw1s[k * P:(k + 1) * P, :]
                    else:
                        srcap = w1s[ex, k * P:(k + 1) * P, 0:FF]
                    wrdy = pr.op("sync", lambda e, k_=k, s_=srcap: e.dma_start(
                        out=wbuf[:, k_ * FF:(k_ + 1) * FF], in_=s_),
                        waits=[("t", W["t"]), ("dg", cast_done)], inc="ds", inc_by=16)
                base_s = W["s"]
                evac_s = [base_s] * NKF
                for i in range(NKF):
                    pb = i % 2
                    def mm_g(e, i_=i, pb_=pb):
                        for k in range(NKD):
                            for Nb in range(2):
                                last = e.matmul(
                                    p8[2 * pb_ + Nb][:, 0:512],
                                    wbuf[:, k * FF + i_ * P:k * FF + (i_ + 1) * P],
                                    h1b[:, k * 1024 + Nb * 512:k * 1024 + Nb * 512 + 512],
                                    start=(k == 0), stop=(k == NKD - 1))
                        return last
                    pr.op("tensor", mm_g,
                          waits=[("ds", wrdy),
                                 ("s", base_s if i < 2 else evac_s[i - 2])],
                          inc="t")
                    tnext = W["t"] + 1
                    for Nb in range(2):
                        pr.op("scalar", lambda e, i_=i, pb_=pb, Nb_=Nb: e.activation(
                            out=sga[:, i_ * 1024 + Nb_ * 512:i_ * 1024 + Nb_ * 512 + 512],
                            in_=p8[2 * pb_ + Nb_][:, 0:512], func=AF.Silu),
                            waits=[("t", tnext)], inc="s")
                    evac_s[i] = W["s"]
                g_all = W["s"]
                pe_flush()
                wrdy = 0
                for k in range(NKD):
                    if shared:
                        srcap = sw2s[k * P:(k + 1) * P, :]
                    else:
                        srcap = w1s[ex, k * P:(k + 1) * P, FF:2 * FF]
                    wrdy = pr.op("sync", lambda e, k_=k, s_=srcap: e.dma_start(
                        out=wbuf[:, k_ * FF:(k_ + 1) * FF], in_=s_),
                        waits=[("t", W["t"])], inc="ds", inc_by=16)
                base_s2 = W["s"]
                base_v2 = W["v"]
                evac_v = [base_v2] * NKF
                for i in range(NKF):
                    pb = i % 2
                    def mm_u(e, i_=i, pb_=pb):
                        for k in range(NKD):
                            for Nb in range(2):
                                last = e.matmul(
                                    p8[2 * pb_ + Nb][:, 0:512],
                                    wbuf[:, k * FF + i_ * P:k * FF + (i_ + 1) * P],
                                    h1b[:, k * 1024 + Nb * 512:k * 1024 + Nb * 512 + 512],
                                    start=(k == 0), stop=(k == NKD - 1))
                        return last
                    pr.op("tensor", mm_u,
                          waits=[("ds", wrdy), ("s", base_s2),
                                 ("v", base_v2 if i < 2 else evac_v[i - 2])],
                          inc="t")
                    tnext = W["t"] + 1
                    for Nb in range(2):
                        pr.op("vector", lambda e, i_=i, pb_=pb, Nb_=Nb: e.tensor_tensor(
                            out=sga[:, i_ * 1024 + Nb_ * 512:i_ * 1024 + Nb_ * 512 + 512],
                            in0=p8[2 * pb_ + Nb_][:, 0:512],
                            in1=sga[:, i_ * 1024 + Nb_ * 512:i_ * 1024 + Nb_ * 512 + 512],
                            op=ALU.mult),
                            waits=([("t", tnext), ("s", g_all)] if i == 0 and Nb == 0
                                   else [("t", tnext)]), inc="v")
                        if not shared:
                            pr.op("vector", lambda e, i_=i, Nb_=Nb: e.tensor_tensor(
                                out=sga[:, i_ * 1024 + Nb_ * 512:i_ * 1024 + Nb_ * 512 + 512],
                                in0=sga[:, i_ * 1024 + Nb_ * 512:i_ * 1024 + Nb_ * 512 + 512],
                                in1=cwbc[:, Nb_ * 512:Nb_ * 512 + 512], op=ALU.mult),
                                inc="v")
                    evac_v[i] = W["v"]
                pe_flush()
                wrdy = 0
                for k in range(NKF):
                    if shared:
                        srcap = sw3s[k * P:(k + 1) * P, :]
                    else:
                        srcap = w2s[ex, k * P:(k + 1) * P, :]
                    wrdy = pr.op("sync", lambda e, k_=k, s_=srcap: e.dma_start(
                        out=wbuf[:, k_ * DM:(k_ + 1) * DM], in_=s_),
                        waits=[("t", W["t"])], inc="ds", inc_by=16)
                for grp in ((0, 1, 2), (3, 4, 5), (6, 7, 8), (9, 10, 11),
                            (12, 13, 14), (15,)):
                    def mm_w2(e, grp_=grp):
                        for gi, dm in enumerate(grp_):
                            for k in range(NKF):
                                for Nb in range(2):
                                    last = e.matmul(
                                        p8[2 * gi + Nb][:, 0:512],
                                        wbuf[:, k * DM + dm * P:k * DM + (dm + 1) * P],
                                        sga[:, k * 1024 + Nb * 512:k * 1024 + Nb * 512 + 512],
                                        start=(k == 0), stop=(k == NKF - 1))
                        return last
                    pr.op("tensor", mm_w2,
                          waits=[("ds", wrdy), ("v", W["v"]), ("s", W["s"])], inc="t")
                    pe_flush()
                    tw2 = W["t"]
                    for gi, dm in enumerate(grp):
                        for Nb in range(2):
                            pr.op("vector", lambda e, gi_=gi, dm_=dm, Nb_=Nb: e.tensor_tensor(
                                out=accb[:, dm_ * 1024 + Nb_ * 512:dm_ * 1024 + Nb_ * 512 + 512],
                                in0=p8[2 * gi_ + Nb_][:, 0:512],
                                in1=accb[:, dm_ * 1024 + Nb_ * 512:dm_ * 1024 + Nb_ * 512 + 512],
                                op=ALU.add),
                                waits=[("t", tw2)], inc="v")
            if dbg_ffn is not None:
                vff = W["v"]
                for k in range(NKD):
                    pr.op("sync", lambda e, k_=k, j_=j: e.dma_start(
                        out=dbg_ffn[k_ * P:(k_ + 1) * P, j_ * 1024:(j_ + 1) * 1024],
                        in_=accb[:, k_ * 1024:(k_ + 1) * 1024]),
                        waits=[("v", vff)], inc="ds", inc_by=16)
            for k in range(NKD):
                pr.op("vector", lambda e, k_=k: e.tensor_tensor(
                    out=accb[:, k_ * 1024:(k_ + 1) * 1024],
                    in0=accb[:, k_ * 1024:(k_ + 1) * 1024],
                    in1=h1b[:, k_ * 1024:(k_ + 1) * 1024], op=ALU.add), inc="v")
            def ln2_write(k, s_bc2):
                pr.op("vector", lambda e, k_=k: e.tensor_tensor(
                    out=of32[:], in0=accb[:, k_ * 1024:(k_ + 1) * 1024],
                    in1=a_bc[:], op=ALU.mult),
                    waits=[("s", s_bc2), ("ds", W["ds"])], inc="v")
                pr.op("vector", lambda e, k_=k: e.tensor_tensor(
                    out=of32[:], in0=of32[:], in1=b_bc[:], op=ALU.add), inc="v")
                vo = W["v"]
                pr.op("sync", lambda e, k_=k, j_=j: e.dma_start(
                    out=out_fm[k_ * P:(k_ + 1) * P, j_ * 1024:(j_ + 1) * 1024],
                    in_=of32[:]),
                    waits=[("v", vo)], inc="ds", inc_by=16)
            emit_ln(lambda k: accb[:, k * 1024:(k + 1) * 1024], ln2_write, j)
        esC.close()
        pr.barrier(dummy)

    with nc.Block() as block:
        em_prologue()
        if "A" in phases:
            em_phaseA()
        if "A2" in phases:
            em_phaseA2()
        if "B" in phases:
            em_phaseB()
        esAB.close()
        if "C" in phases:
            em_phaseC()
        pr.op("sync", lambda e: e.nop(), waits=[("ds", W["ds"]), ("dg", W["dg"])])
        pr.run_blocks(block)
    es.close()
    return nc



def _prep_core_inputs(inputs, b, h):
    C0 = h * CH
    asC = np.ascontiguousarray
    x = inputs["L0_attn_x"]
    W_in = inputs["L0_attn_in_proj_w"]

    def pack_vec(v):
        return asC(v.reshape(NCG, P).T)

    d = {
        "xT": asC(x[b].T),
        "xT_res": asC(x[b, h * LH:(h + 1) * LH, :].T),
        "w_in_T": asC(np.concatenate(
            [W_in[C0:C0 + CH], W_in[INNER + C0:INNER + C0 + CH]], axis=0).T),
        "w_xp_T": asC(inputs["L0_attn_x_proj_w"][:, C0:C0 + CH].T),
        "w_dt_T": asC(inputs["L0_attn_dt_proj_w"][C0:C0 + CH, :].T),
        "w_out_T": asC(inputs["L0_attn_out_proj_w"][:, C0:C0 + CH].T),
        "gate_T": asC(inputs["L0_ffn_gate_w"].T),
        "w1": inputs["L0_ffn_w1"],
        "w2": inputs["L0_ffn_w2"],
        "sw1_T": asC(inputs["L0_ffn_sw1"].T),
        "sw2_T": asC(inputs["L0_ffn_sw2"].T),
        "sw3_T": asC(inputs["L0_ffn_sw3"].T),
        "v_dtb": pack_vec(inputs["L0_attn_dt_proj_b"][C0:C0 + CH]),
        "v_convb": pack_vec(inputs["L0_attn_conv_b"][C0:C0 + CH]),
        "v_convw": asC(inputs["L0_attn_conv_w"][C0:C0 + CH, 0, :]
                       .reshape(NCG, P, 4).transpose(1, 0, 2).reshape(P, NCG * 4)),
        "v_alog": asC(inputs["L0_attn_A_log"][C0:C0 + CH]
                      .reshape(NCG, P, NST).transpose(1, 0, 2).reshape(P, NCG * NST)),
        "v_D": pack_vec(inputs["L0_attn_D"][C0:C0 + CH]),
    }
    return {k: np.ascontiguousarray(v) for k, v in d.items()}


def kernel(**inputs):
    nc = build_program()
    in_maps = []
    for b in range(4):
        for h in range(2):
            in_maps.append(_prep_core_inputs(inputs, b, h))
    res = run_bass_kernel_spmd(nc, in_maps, core_ids=list(range(8)))
    out = np.empty((4, L, DM), np.float32)
    for b in range(4):
        for h in range(2):
            out[b, h * LH:(h + 1) * LH, :] = res.results[b * 2 + h]["out_fm"].T
    return out
